# version 01
# baseline (speedup 1.0000x reference)
import math
import os

import numpy as np
import ml_dtypes
import concourse.bass as bass
import concourse.mybir as mybir
from concourse.ap import AP
from concourse.bass_utils import run_bass_kernel_spmd

F32 = mybir.dt.float32
BF16 = mybir.dt.bfloat16
AX = mybir.AxisListType.X
ALU = mybir.AluOpType
ACT = mybir.ActivationFunctionType

D = 1024
DH = 4096
T = 8192
ST = 512
NST = T // ST
QT = 128
NQT = ST // QT
KEYS = 161
SUB = 256
NSUB = ST // SUB
NG = 21
SCALE = 0.125
NEG = -1.0e30
RING = ST + 32
R2 = ST + 2

C32 = float(np.float32(1.5 * 2.0 ** 23))
C52 = float(np.float32(1.5 * 2.0 ** 52))
LN127 = float(math.log(127.0))
NW1 = float(DH * D)

DEBUG = []
RACEDET = os.environ.get("RACEDET", "") != ""
SAFE = int(os.environ.get("SAFE", "0"))

ENG = ("t", "v", "s", "g", "y")
DMAQ = ("dcn", "dp0", "dp1", "dx0", "dx1", "dqk0", "dqk1", "dv0", "dv1",
        "dwo0", "dwo1", "dw10", "dw11", "dw20", "dw21", "ds0", "ds1", "dqq", "dout",
        "dla0", "dla1", "dlb0", "dlb1", "dph0", "dph1", "dmm0", "dmm1",
        "dmg0", "dmg1")


class Bld:

    def __init__(self):
        self.prog = {e: [] for e in ENG}
        self.cnt = {n: 0 for n in ENG + DMAQ}
        self.hwm = {e: {} for e in ENG}
        self.sems = None

    def _mkwait(self, eng, waits):
        ws = {}
        for tok in waits:
            if tok is None:
                continue
            sname, val = tok
            if val > ws.get(sname, 0):
                ws[sname] = val
        hw = self.hwm[eng]
        todo = []
        for sname, val in ws.items():
            if hw.get(sname, 0) < val:
                hw[sname] = val
                todo.append((sname, val))

        def f(code):
            for sname, val in todo:
                code.wait_ge(self.sems[sname], val)
        return f

    def op(self, eng, fn, waits=(), inc=True, tag=""):
        wfn = self._mkwait(eng, waits)
        if inc:
            self.cnt[eng] += 1
            val = self.cnt[eng]

            def run(code, wfn=wfn, fn=fn, sn=eng, tag=tag):
                wfn(code)
                ins = fn(code)
                assert ins is not None, f"op {tag} returned None"
                ins.then_inc(self.sems[sn], 1)
            self.prog[eng].append(run)
            return (eng, val)
        else:
            def run(code, wfn=wfn, fn=fn):
                wfn(code)
                fn(code)
            self.prog[eng].append(run)
            return None

    def barrier(self):
        toks = [(n, v) for n, v in self.cnt.items() if v > 0]
        for e in ENG:
            if e == "t":
                continue
            self.op(e, lambda c: c.nop(), waits=toks, inc=True)

    def dma(self, eng, q, out_ap, in_ap, waits=(), tag="", slow=False):
        wfn = self._mkwait(eng, waits)
        self.cnt[q] += 16
        val = self.cnt[q]

        def run(code, wfn=wfn, o=out_ap, i=in_ap, q=q, slow=slow):
            wfn(code)
            code.dma_start(out=o, in_=i,
                           allow_slow_non_contiguous=slow).then_inc(
                self.sems[q], 16)
        self.prog[eng].append(run)
        return (q, val)


def vap(t, offset, dims):
    return AP(t[:].tensor, offset, dims)


class NS:
    pass


def build_nc(debug=()):
    nc = bass.Bass("TRN2", target_bir_lowering=False,
                   detect_race_conditions=RACEDET)
    E = NS()
    E.nc = nc

    E.x_in = nc.dram_tensor("x", [D, T], F32, kind="ExternalInput").ap()
    E.wqkv_in = nc.dram_tensor("wqkv", [D, 3 * D], F32, kind="ExternalInput").ap()
    E.wout_in = nc.dram_tensor("wout", [D, D], F32, kind="ExternalInput").ap()
    E.bout_in = nc.dram_tensor("bout", [128, 8], F32, kind="ExternalInput").ap()
    E.wtht_in = nc.dram_tensor("wtht", [16, 16], F32, kind="ExternalInput").ap()
    E.w1t_in = nc.dram_tensor("w1t", [D, DH], F32, kind="ExternalInput").ap()
    E.w2t_in = nc.dram_tensor("w2t", [DH, D], F32, kind="ExternalInput").ap()
    E.convw_in = nc.dram_tensor("convw", [128, 96], F32, kind="ExternalInput").ap()
    E.convb_in = nc.dram_tensor("convb", [128, 32], F32, kind="ExternalInput").ap()
    E.maskA_in = nc.dram_tensor("maskA", [128, 128], BF16, kind="ExternalInput").ap()
    E.maskB_in = nc.dram_tensor("maskB", [128, 33], BF16, kind="ExternalInput").ap()
    E.maskA0_in = nc.dram_tensor("maskA0", [128, 128], BF16, kind="ExternalInput").ap()
    E.onesb_in = nc.dram_tensor("onesb", [128, 128], BF16, kind="ExternalInput").ap()
    E.onesf_in = nc.dram_tensor("onesf", [1, 128], F32, kind="ExternalInput").ap()
    E.identb_in = nc.dram_tensor("identb", [128, 128], BF16, kind="ExternalInput").ap()
    E.identf_in = nc.dram_tensor("identf", [128, 128], F32, kind="ExternalInput").ap()
    E.consts_in = nc.dram_tensor("consts", [128, 8], F32, kind="ExternalInput").ap()
    E.out = nc.dram_tensor("o", [D, T], F32, kind="ExternalOutput").ap()

    E.debug = tuple(debug)

    E.s_wqk = nc.dram_tensor("s_wqk", [16 * 8 * 16384], BF16).ap()
    E.s_wv = nc.dram_tensor("s_wv", [8 * 131072], BF16).ap()
    E.s_wo = nc.dram_tensor("s_wo", [8 * 8 * 16384], BF16).ap()
    E.s_w1 = nc.dram_tensor("s_w1", [32 * 8 * 16384], BF16).ap()
    E.s_w2 = nc.dram_tensor("s_w2", [32 * 8 * 16384], BF16).ap()
    E.d_e = [nc.dram_tensor(f"d_e{i}", [168 * 2048], BF16).ap() for i in range(2)]
    E.d_m = [nc.dram_tensor(f"d_m{i}", [128 * 2688], BF16).ap() for i in range(2)]
    E.d_vb = nc.dram_tensor("d_vb", [1024], BF16).ap()


    ctx = []

    def sbuf(name, shape, dt):
        g = nc.sbuf_tensor(name, shape, dt)
        t = g.__enter__()
        ctx.append(g)
        return t

    E.xb = [sbuf(f"xb{i}", [128, 8 * ST], F32) for i in range(2)]
    E.x2r = sbuf("x2r", [128, 8 * R2], F32)
    E.xnr = sbuf("xnr", [128, 8 * RING], BF16)
    E.xq1r = sbuf("xq1r", [128, 8 * R2], BF16)
    E.arena = sbuf("arena", [128, 2 * 8 * ST], BF16)
    E.qfm = E.arena
    E.krng = sbuf("krng", [128, 8 * RING], BF16)
    E.vsl = sbuf("vsl", [128, 5 * 1024], BF16)
    E.kst = sbuf("kst", [128, 8 * KEYS], BF16)
    E.stat = E.arena
    E.m_bc = sbuf("m_bc", [128, ST], F32)
    E.r_bc = sbuf("r_bc", [128, ST], F32)
    E.a_bc = sbuf("a_bc", [128, ST], F32)
    E.s_bc = sbuf("s_bc", [128, ST], F32)
    E.alr = sbuf("alr", [128, R2], F32)
    E.beta = sbuf("beta", [128, SUB], F32)
    E.s2bc = sbuf("s2bc", [128, SUB], F32)
    E.eA = sbuf("eA", [128, 16 * 128], BF16)
    E.eB = sbuf("eB", [40, 16 * 128], BF16)
    E.PH = sbuf("PH", [128, NG * 128], BF16)
    E.Msb = sbuf("Msb", [128, NG * 128], BF16)
    E.MgA = sbuf("MgA", [128, 16 * 128], BF16)
    E.MgB = sbuf("MgB", [32, 16 * 128], BF16)
    E.Mgb = sbuf("Mgb", [8, 16 * 128], BF16)
    E.negL = sbuf("negL", [16, 8 * 128], F32)
    E.atto = E.arena
    E.hbuf = sbuf("hbuf", [128, 32 * (SUB + 2)], F32)
    E.accr = sbuf("accr", [128, SUB], F32)
    E.xq2 = E.arena
    E.wbqk = [sbuf(f"wbqk{i}", [128, 1024], BF16) for i in range(2)]
    E.wbv = [sbuf(f"wbv{i}", [128, 1024], BF16) for i in range(2)]
    E.wbo = [sbuf(f"wbo{i}", [128, 1024], BF16) for i in range(2)]
    E.wb1 = [sbuf(f"wb1{i}", [128, 1024], BF16) for i in range(2)]
    E.wb2 = [sbuf(f"wb2{i}", [128, 1024], BF16) for i in range(2)]
    E.maskA = sbuf("smaskA", [128, 128], BF16)
    E.maskB = sbuf("smaskB", [128, 33], BF16)
    E.maskA0 = sbuf("smaskA0", [128, 128], BF16)
    E.onesb = sbuf("sonesb", [128, 128], BF16)
    E.onesf = sbuf("sonesf", [1, 128], F32)
    E.identb = sbuf("sidentb", [128, 128], BF16)
    E.identf = sbuf("sidentf", [128, 128], F32)
    E.wbig = sbuf("wbig", [128, 128], BF16)
    E.wtht = sbuf("swtht", [16, 16], F32)
    E.wthtb = sbuf("swthtb", [16, 16], BF16)
    E.kbos = sbuf("kbos", [128, 8], BF16)
    E.vbos = sbuf("vbos", [1, 1024], BF16)
    E.vbosT = sbuf("vbosT", [128, 8], BF16)
    E.bout = sbuf("sbout", [128, 8], F32)
    E.convw = sbuf("sconvw", [128, 96], F32)
    E.convb = sbuf("sconvb", [128, 32], F32)
    E.consts = sbuf("sconsts", [128, 8], F32)
    E.cdom = sbuf("cdom", [128, 1], F32)
    E.lnc1 = sbuf("lnc1", [128, 1], F32)
    E.lnc2 = sbuf("lnc2", [128, 1], F32)
    E.swsc = sbuf("swsc", [128, 4], F32)
    E.smal = sbuf("smal", [128, 16], F32)
    E.x0c = sbuf("x0c", [128, 8], F32)
    E.xn0 = sbuf("xn0", [128, 8], BF16)

    psg_g = nc.psum_tensor("PS", [128, 4096], F32)
    E.ps = psg_g.__enter__()
    ctx.append(psg_g)

    E.taps = {}
    E.tapsrc = {}
    for name in E.debug:
        t = getattr(E, name, None)
        if t is None:
            shp = list(DEBUG_SHAPES[name])
            dt = BF16 if name in ("bvT", "bq1") else F32
            E.taps[name] = nc.dram_tensor("tap_" + name, shp, dt,
                                          kind="ExternalOutput").ap()
            E.tapsrc[name] = None
            continue
        shp = list(t[:].shape)
        E.taps[name] = nc.dram_tensor("tap_" + name, shp, t[:].dtype,
                                      kind="ExternalOutput").ap()
        E.tapsrc[name] = t

    b = Bld()
    sems = {}
    for sname in ENG + DMAQ:
        g = nc.semaphore("sem_" + sname)
        sems[sname] = g.__enter__()
        ctx.append(g)
    b.sems = sems
    E.b = b

    emit_all(E)

    blk_g = nc.Block()
    blk = blk_g.__enter__()
    ctx.append(blk_g)
    for ename, attr in (("t", "tensor"), ("v", "vector"), ("s", "scalar"),
                        ("g", "gpsimd"), ("y", "sync")):
        def body(code, ename=ename):
            for fn in b.prog[ename]:
                fn(code)
        getattr(blk, attr)(body)

    for g in reversed(ctx):
        g.__exit__(None, None, None)
    return nc


DEBUG_SHAPES = {
    "xnr": (128, 8 * RING),
    "qfm": (128, 8 * ST),
    "krng": (128, 8 * RING),
    "vsl": (128, 5 * 1024),
    "eA": (128, 16 * 128),
    "eB": (40, 16 * 128),
    "MgA": (128, 16 * 128),
    "MgB": (32, 16 * 128),
    "Mgb": (8, 16 * 128),
    "atto": (128, 8 * ST),
    "x2r": (128, 8 * R2),
    "alr": (128, R2),
    "xq1r": (128, 8 * R2),
    "hbuf": (128, 32 * (SUB + 2)),
    "xq2": (128, 32 * SUB),
    "kbos": (128, 8),
    "vbos": (1, 1024),
    "wbig": (128, 128),
    "PH": (128, NG * 128),
    "Msb": (128, NG * 128),
    "x0c": (128, 8),
    "xn0": (128, 8),
    "bx2": (128, 8),
    "bq1": (128, 8),
    "bal": (128, 1),
    "bvT": (128, 8),
    "bsm": (128, 16),
}

TAPQ = {BF16: None}


def bank(E, i, c0=0, c1=512):
    return E.ps[:, i * 512 + c0: i * 512 + c1]


def emit_all(E):
    b = E.b
    st8 = {
        "bank": [("t", 0)] * 8,
        "xtok": [None, None],
        "xfree": [("t", 0), ("t", 0)],
        "out_toks": [],
        "vslot": [("t", 0)] * 5,
        "vslot_free": [("t", 0)] * 5,
    }
    E.state = st8
    emit_setup(E)
    emit_prep(E)
    emit_bos_ln0(E)
    st8["xtok"][0] = load_x(E, 0)
    for s in range(NST):
        emit_supertile(E, s)
    for name in E.taps:
        if E.tapsrc[name] is None:
            continue
        tt = b.dma("y", "dout", E.taps[name], E.tapsrc[name][:, :],
                   waits=list(st8["out_toks"]), tag="tap_" + name)
        st8["out_toks"].append(tt)
    b.op("y", lambda c: c.nop(), waits=st8["out_toks"], inc=False)


def load_x(E, s):
    b = E.b
    buf = E.xb[s % 2]
    lo = 512 * s + 1
    n = ST if s < NST - 1 else ST - 1
    w = [E.state["xfree"][s % 2]]
    toks = []
    for c in range(8):
        src = E.x_in[c * 128:(c + 1) * 128, lo:lo + n]
        toks.append(b.dma("y", "dx" + str(s % 2), buf[:, c * ST: c * ST + n], src, waits=w,
                          tag=f"ldx{s}"))
    if n < ST:
        toks.append(b.op("v", lambda c: c.memset(
            vap(E.xb[s % 2], ST - 1, [[8 * ST, 128], [ST, 8], [1, 1]]), 0.0),
            waits=w + [toks[-1]], tag="padx"))
    return toks[-1]


def emit_setup(E):
    b = E.b
    t_c = b.dma("y", "dcn", E.consts[:], E.consts_in)
    for dst, src in [(E.maskA, E.maskA_in), (E.maskB, E.maskB_in),
                     (E.maskA0, E.maskA0_in), (E.onesb, E.onesb_in),
                     (E.onesf, E.onesf_in), (E.identb, E.identb_in), (E.identf, E.identf_in),
                     (E.wtht, E.wtht_in), (E.bout, E.bout_in),
                     (E.convw, E.convw_in), (E.convb, E.convb_in)]:
        t_last = b.dma("y", "dcn", dst[:], src)
    E.tok_consts = t_last

    t_p = b.op("v", lambda c: c.tensor_scalar(
        E.smal[:, 0:1], E.consts[:, 0:1], C52, C52, ALU.add, ALU.subtract),
        waits=[E.tok_consts], tag="probe")
    E.tok_cdom = b.op("v", lambda c: c.tensor_scalar(
        E.cdom[:], E.smal[:, 0:1], C52 - C32, C32, ALU.mult, ALU.add),
        waits=[t_p], tag="cdom")

    t_wb = b.op("v", lambda c: c.memset(E.wbig[:], 0.0), waits=[E.tok_consts])
    t_wbc = b.op("v", lambda c: c.tensor_copy(E.wthtb[:], E.wtht[:]),
                 waits=[E.tok_consts])
    for j in range(8):
        dst = vap(E.wbig, (j * 16) * 128 + j * 16, [[128, 16], [1, 16]])
        t_wb = b.dma("y", "dcn", dst, E.wthtb[:, :],
                     waits=[t_wb, t_wbc], tag="wbig", slow=True)
    E.tok_wbig = t_wb
    E.tok_consts = t_wb

    E.tok_eBz = b.op("v", lambda c: c.memset(E.eB[:], 0.0))
    E.tok_krz = b.op("v", lambda c: c.memset(
        vap(E.krng, 0, [[8 * RING, 128], [RING, 8], [1, 32]]), 0.0))
    E.tok_xnz = b.op("v", lambda c: c.memset(
        vap(E.xnr, 0, [[8 * RING, 128], [RING, 8], [1, 32]]), 0.0))
    E.tok_xq1z = b.op("v", lambda c: c.memset(
        vap(E.xq1r, 0, [[8 * R2, 128], [R2, 8], [1, 2]]), 0.0))
    E.tok_x2z = b.op("v", lambda c: c.memset(
        vap(E.x2r, 0, [[8 * R2, 128], [R2, 8], [1, 2]]), 0.0))
    E.tok_alz = b.op("v", lambda c: c.memset(E.alr[:, 0:2], 0.0))


def emit_prep(E):
    b = E.b
    ps = E.ps
    E.prep_last = [None, None]

    def absmean(src, nchunk, cols, out_col, tag):
        acc = None
        half_busy = list(E.prep_last)
        for ch in range(nchunk):
            hb = ch % 2
            buf = E.xb[hb][:, 0:cols]
            w = [x for x in (half_busy[hb],) if x]
            t_ld = b.dma("y", "dp" + str(hb), buf, src[ch * 128:(ch + 1) * 128, :],
                         waits=w, tag=f"{tag}l{ch}")
            t_rd = b.op("v", lambda c, buf=buf, hb=hb: c.reduce_sum(
                E.smal[:, 8 + hb:9 + hb], buf, AX, apply_absolute_value=True),
                waits=[t_ld], tag=f"{tag}r{ch}")
            half_busy[hb] = t_rd
            E.prep_last[hb] = t_rd
            if ch == 0:
                acc = b.op("v", lambda c: c.tensor_copy(
                    E.smal[:, out_col:out_col + 1], E.smal[:, 8:9]),
                    waits=[t_rd])
            else:
                acc = b.op("v", lambda c, hb=hb: c.tensor_add(
                    E.smal[:, out_col:out_col + 1],
                    E.smal[:, out_col:out_col + 1], E.smal[:, 8 + hb:9 + hb]),
                    waits=[t_rd, acc])
        return acc

    t_a1 = absmean(E.w1t_in, 8, DH, 10, "a1")
    t_a2 = absmean(E.w2t_in, 32, D, 11, "a2")

    t_ps = b.op("t", lambda c: c.matmul(
        bank(E, 7, 0, 2)[0:1, :], E.consts[:, 0:1], E.smal[:, 10:12],
        start=True, stop=True), waits=[t_a1, t_a2, E.tok_consts], tag="psum_w")
    t_m = b.op("v", lambda c: c.tensor_scalar(
        E.smal[0:1, 12:14], bank(E, 7, 0, 2)[0:1, :], 1.0 / NW1, 1e-5,
        ALU.mult, ALU.max), waits=[t_ps], tag="wmean")
    t_r = b.op("v", lambda c: c.reciprocal(
        E.smal[0:1, 14:16], E.smal[0:1, 12:14]), waits=[t_m])
    t_bc = b.op("t", lambda c: c.matmul(
        bank(E, 7, 16, 20), E.onesf[0:1, :], E.smal[0:1, 12:16],
        start=True, stop=True), waits=[t_r, E.tok_consts], tag="bc_sw")
    t_sw = b.op("v", lambda c: c.tensor_copy(
        E.swsc[:, :], vap(E.ps, 7 * 512 + 16, [[4096, 128], [1, 4]])),
        waits=[t_bc], tag="swsc")
    t_l1 = b.op("s", lambda c: c.activation(
        E.smal[:, 4:6], E.swsc[:, 0:2], ACT.Ln, bias=0.0, scale=1.0),
        waits=[t_sw], tag="lnc")
    t_l2 = b.op("v", lambda c: c.tensor_scalar(
        E.lnc1[:], E.smal[:, 4:5], LN127, None, ALU.subtract), waits=[t_l1])
    t_l3 = b.op("v", lambda c: c.tensor_scalar(
        E.lnc2[:], E.smal[:, 5:6], LN127, None, ALU.subtract), waits=[t_l1])
    E.tok_lnc = (t_l2, t_l3)

    def quant_store(src, nchunk, cols, scol, scratch, dst_dims_fn, tag):
        half_busy = list(E.prep_last)
        last = getattr(E, "prep_store_last", None)
        for ch in range(nchunk):
            hb = ch % 2
            buf = E.xb[hb][:, 0:cols]
            w = [x for x in (half_busy[hb],) if x]
            t_ld = b.dma("y", "dp" + str(hb), buf, src[ch * 128:(ch + 1) * 128, :],
                         waits=w, tag=f"{tag}ql{ch}")
            t1 = b.op("v", lambda c, buf=buf: c.tensor_scalar(
                buf, buf, E.swsc[:, scol:scol + 1], E.cdom[:],
                ALU.mult, ALU.add), waits=[t_ld, t_sw, E.tok_cdom],
                tag=f"{tag}q1")
            t2 = b.op("v", lambda c, buf=buf: c.tensor_scalar(
                buf, buf, E.cdom[:], -1.0, ALU.subtract, ALU.max),
                waits=[t1], tag=f"{tag}q2")
            t3 = b.op("v", lambda c, buf=buf, cols=cols: c.tensor_scalar(
                E.stat[:, 0:cols], buf, 1.0, None, ALU.min),
                waits=[t2, last], tag=f"{tag}q3")
            half_busy[hb] = t3
            E.prep_last[hb] = t3
            dst, sdims = dst_dims_fn(ch)
            last = b.dma("y", "dout", dst,
                         vap(E.stat, 0, sdims), waits=[t3], tag=f"{tag}qs{ch}")
        E.prep_store_last = last
        return last

    def w1dst(dc):
        dst = vap_dram(E.s_w1, dc * 128,
                       [[1024, 128], [131072, 32], [1, 128]])
        return dst, [[2 * 8 * ST, 128], [128, 32], [1, 128]]
    t_w1 = quant_store(E.w1t_in, 8, DH, 2, E.s_w1, w1dst, "w1")

    def w2dst(hc):
        dst = vap_dram(E.s_w2, hc * 131072,
                       [[1024, 128], [128, 8], [1, 128]])
        return dst, [[2 * 8 * ST, 128], [128, 8], [1, 128]]
    t_w2 = quant_store(E.w2t_in, 32, D, 3, E.s_w2, w2dst, "w2")

    def cast_store(src, nchunk, cols, dst_fn, tag):
        half_busy = list(E.prep_last)
        last = getattr(E, "prep_store_last", None)
        for ch in range(nchunk):
            hb = ch % 2
            buf = E.xb[hb][:, 0:cols]
            w = [x for x in (half_busy[hb],) if x]
            t_ld = b.dma("y", "dp" + str(hb), buf, src[ch * 128:(ch + 1) * 128, :],
                         waits=w, tag=f"{tag}cl{ch}")
            t1 = b.op("v", lambda c, buf=buf, cols=cols: c.tensor_copy(
                E.stat[:, 0:cols], buf), waits=[t_ld, last], tag=f"{tag}cc")
            half_busy[hb] = t1
            E.prep_last[hb] = t1
            for dst, sdims, soff in dst_fn(ch):
                last = b.dma("y", "dout", dst, vap(E.stat, soff, sdims),
                             waits=[t1], tag=f"{tag}cs{ch}")
        E.prep_store_last = last
        return last

    def qkvdst(dc):
        out = []
        out.append((vap_dram(E.s_wqk, dc * 128,
                             [[1024, 128], [131072, 16], [1, 128]]),
                    [[2 * 8 * ST, 128], [128, 16], [1, 128]], 0))
        out.append((vap_dram(E.s_wv, dc * 131072, [[1024, 128], [1, 1024]]),
                    [[2 * 8 * ST, 128], [1, 1024]], 2048))
        return out
    t_qkv = cast_store(E.wqkv_in, 8, 3 * D, qkvdst, "qkv")

    def wodst(ic):
        return [(vap_dram(E.s_wo, ic * 128,
                          [[1024, 128], [131072, 8], [1, 128]]),
                 [[2 * 8 * ST, 128], [128, 8], [1, 128]], 0)]
    t_wo = cast_store(E.wout_in, 8, D, wodst, "wo")

    E.tok_prep = [t_w1, t_w2, t_qkv, t_wo]
    E.tok_stat_prep = t_wo
    E.state["bank"][7] = t_sw
    E.state["xfree"][0] = E.prep_last[0]
    E.state["xfree"][1] = E.prep_last[1]


def vap_dram(t, offset, dims):
    return AP(t.tensor, offset, dims)


def emit_supertile(E, s):
    b = E.b
    st = E.state
    if SAFE:
        b.barrier()
    if s + 1 < NST:
        st["xtok"][(s + 1) % 2] = load_x(E, s + 1)
    if SAFE:
        b.barrier()
    emit_ln1_qkv(E, s)
    for jl in range(NQT):
        if SAFE:
            b.barrier()
        emit_attention_qtile(E, s, jl)
    if s == 0:
        emit_bos_kv(E)
    if SAFE:
        b.barrier()
    emit_outproj(E, s)
    if s == NST - 1 and "atto" in E.taps:
        tt = b.dma("y", "dout",
                   vap_dram(E.taps.pop("atto"), 0, [[8192, 128], [1, 4096]]),
                   E.atto[:, 4096:8192],
                   waits=list(E.tok_atto), tag="tap_atto")
        E.state["out_toks"].append(tt)
    if s == 0:
        emit_bos_x2(E)
    if SAFE:
        b.barrier()
    emit_ln2_quant1(E, s)
    if SAFE:
        b.barrier()
    if s == 0:
        emit_bos_quant1(E)
        if SAFE:
            b.barrier()
        for nm, src_ap, wt in (
                ("bx2", vap(E.x2r, 1, [[8 * R2, 128], [R2, 8], [1, 1]]),
                 E.tok_x2bos),
                ("bq1", vap(E.xq1r, 1, [[8 * R2, 128], [R2, 8], [1, 1]]),
                 [E.tok_xq1bos]),
                ("bal", E.alr[:, 1:2], [E.tok_albos]),
                ("bvT", E.vbosT[:, :], [E.tok_vbosT])):
            if nm in E.taps:
                tt = b.dma("y", "dout", E.taps.pop(nm), src_ap,
                           waits=list(wt) if isinstance(wt, list) else wt,
                           tag="tap_" + nm, slow=True)
                E.state["out_toks"].append(tt)
    if s == NST - 1:
        t_pz = b.op("v", lambda c: c.memset(
            vap(E.xq1r, 513, [[8 * R2, 128], [R2, 8], [1, 1]]), 0.0),
            waits=E.tok_xq1, tag="padz")
        E.tok_xq1 = E.tok_xq1 + [t_pz]
    for u in range(NSUB):
        if SAFE:
            b.barrier()
        emit_ffn_subtile(E, s, u)
    if SAFE:
        b.barrier()
    emit_tails(E, s)


def emit_ln1_qkv(E, s):
    b, ps = E.b, E.ps
    x = E.xb[s % 2]
    xtok = E.state["xtok"][s % 2]
    wst = [E.tok_stat_prep] if s == 0 else [E.tok_stat_free]
    wx2 = [E.tok_x2z] if s == 0 else []

    t_cast = b.op("v", lambda c: c.tensor_copy(E.stat[:, 0:4096], x[:, :]),
                  waits=[xtok] + wst +
                  [getattr(E, "tok_xq2_free", None),
                   getattr(E, "tok_qfm_free", None)], tag="ln1cast")
    mmtoks = []
    for c8 in range(8):
        mmtoks.append(b.op("t", lambda c, c8=c8: c.matmul(
            bank(E, 0), E.onesb[:, :], E.stat[:, c8 * ST:(c8 + 1) * ST],
            start=(c8 == 0), stop=(c8 == 7)),
            waits=[t_cast, E.state["bank"][0], E.tok_consts], tag="ln1s1"))
    t_s1 = mmtoks[-1]
    t_m = b.op("v", lambda c: c.tensor_scalar(
        E.m_bc[:], bank(E, 0), 1.0 / 1024.0, None, ALU.mult),
        waits=[t_s1], tag="ln1m")
    E.state["bank"][0] = t_m
    t_sq = b.op("v", lambda c: c.tensor_tensor(
        E.stat[:, 0:4096], x[:, :], x[:, :], ALU.mult), waits=[t_s1],
        tag="ln1sq")
    for c8 in range(8):
        t_s2 = b.op("t", lambda c, c8=c8: c.matmul(
            bank(E, 1), E.onesb[:, :], E.stat[:, c8 * ST:(c8 + 1) * ST],
            start=(c8 == 0), stop=(c8 == 7)),
            waits=[t_sq, E.state["bank"][1]], tag="ln1s2")
    t_v0 = b.op("v", lambda c: c.tensor_scalar(
        E.s_bc[:], bank(E, 1), 1.0 / 1024.0, None, ALU.mult), waits=[t_s2])
    E.state["bank"][1] = t_v0
    t_mm = b.op("v", lambda c: c.tensor_tensor(
        E.a_bc[:], E.m_bc[:], E.m_bc[:], ALU.mult), waits=[t_m])
    t_var = b.op("v", lambda c: c.tensor_tensor(
        E.s_bc[:], E.s_bc[:], E.a_bc[:], ALU.subtract), waits=[t_v0, t_mm])
    t_lv = b.op("s", lambda c: c.activation(
        E.a_bc[:], E.s_bc[:], ACT.Ln, bias=E.consts[:, 1:2], scale=1.0),
        waits=[t_var], tag="ln1lv")
    t_rs = b.op("s", lambda c: c.activation(
        E.r_bc[:], E.a_bc[:], ACT.Exp, bias=0.0, scale=-0.5),
        waits=[t_lv], tag="ln1rs")

    t_xn = []
    for c8 in range(8):
        t1 = b.op("v", lambda c, c8=c8: c.tensor_tensor(
            E.s_bc[:], x[:, c8 * ST:(c8 + 1) * ST], E.m_bc[:], ALU.subtract),
            waits=[t_m, t_lv, (t_xn[-1] if t_xn else ("t", 0))], tag="ln1a")
        t2 = b.op("v", lambda c, c8=c8: c.tensor_tensor(
            E.xnr[:, c8 * RING + 32: c8 * RING + 32 + ST], E.s_bc[:],
            E.r_bc[:], ALU.mult),
            waits=[t1, t_rs] + ([E.tok_xnz] if s == 0 else []), tag="ln1b")
        t_xn.append(t2)
    tok_xn = t_xn[-1]
    E.tok_stat_free = t_s2

    qk_ev = []
    prev_use = [None, None]
    for fc in range(16):
        buf = E.wbqk[fc % 2]
        w = ([prev_use[fc % 2]] if prev_use[fc % 2] else []) + \
            ([t for t in E.tok_prep] if s == 0 and fc < 2 else [])
        t_ld = b.dma("y", "dqk" + str(fc % 2), buf[:, :],
                     vap_dram(E.s_wqk, fc * 8 * 16384, [[1024, 128], [1, 1024]]),
                     waits=w, tag=f"qkl{fc}")
        bk = 2 + (fc % 2)
        for dc in range(8):
            t_mm = b.op("t", lambda c, fc=fc, dc=dc, buf=buf, bk=bk: c.matmul(
                bank(E, bk), buf[:, dc * 128:(dc + 1) * 128],
                E.xnr[:, dc * RING + 32: dc * RING + 32 + ST],
                start=(dc == 0), stop=(dc == 7)),
                waits=[t_ld, tok_xn, E.state["bank"][bk]], tag=f"qk{fc}")
        prev_use[fc % 2] = t_mm
        if fc < 8:
            dst = E.qfm[:, fc * ST:(fc + 1) * ST]
        else:
            dst = E.xnr
            dst = E.krng[:, (fc - 8) * RING + 32: (fc - 8) * RING + 32 + ST]
        t_ev = b.op("v", lambda c, dst=dst, bk=bk: c.tensor_copy(
            dst, bank(E, bk)), waits=[t_mm, E.tok_stat_free],
            tag=f"qkev{fc}")
        qk_ev.append(t_ev)
        E.state["bank"][bk] = t_ev
        if s == 0 and fc >= 8:
            for dc in range(8):
                t_bk = b.op("t", lambda c, dc=dc, buf=buf: c.matmul(
                    bank(E, 4, 0, 1), buf[:, dc * 128:(dc + 1) * 128],
                    E.xn0[:, dc:dc + 1], start=(dc == 0), stop=(dc == 7)),
                    waits=[E.tok_xn0, E.state["bank"][4]], tag="bosk")
            t_bke = b.op("v", lambda c, fc=fc: c.tensor_copy(
                E.kbos[:, fc - 8:fc - 7], bank(E, 4, 0, 1)), waits=[t_bk])
            E.state["bank"][4] = t_bke
            prev_use[fc % 2] = t_bk
            if fc == 15:
                E.tok_kbos = t_bke
    E.tok_qfm = qk_ev[7]
    E.tok_krng = qk_ev[15]

    prev_v = [None, None]
    vmm = []
    for dc in range(8):
        buf = E.wbv[dc % 2]
        w = [prev_v[dc % 2]] if prev_v[dc % 2] else []
        t_ld = b.dma("y", "dv" + str(dc % 2), buf[:, :],
                     vap_dram(E.s_wv, dc * 131072, [[1024, 128], [1, 1024]]),
                     waits=w, tag=f"vl{dc}")
        for bl in range(4):
            for nn in range(2):
                bk = 2 * bl + nn
                t_mm = b.op("t", lambda c, dc=dc, bl=bl, nn=nn, buf=buf, bk=bk:
                            c.matmul(
                    bank(E, bk), E.xnr[:, dc * RING + bl * 128: dc * RING + bl * 128 + 128],
                    buf[:, nn * 512:(nn + 1) * 512],
                    start=(dc == 0), stop=(dc == 7)),
                    waits=[t_ld, tok_xn,
                           E.state["bank"][bk] if dc == 0 else ("t", 0),
                           E.tok_krz if s == 0 else ("t", 0)],
                    tag=f"v{dc}_{bl}")
                if dc == 7:
                    vmm.append(t_mm)
        prev_v[dc % 2] = t_mm
    for bl in range(4):
        gb = s * 4 + bl
        slot = gb % 5
        for nn in range(2):
            bk = 2 * bl + nn
            t_ev = b.op("v", lambda c, slot=slot, nn=nn, bk=bk: c.tensor_copy(
                E.vsl[:, slot * 1024 + nn * 512: slot * 1024 + (nn + 1) * 512],
                bank(E, bk)),
                waits=[vmm[2 * bl + nn], E.state["vslot_free"][slot]],
                tag=f"vev{bl}")
            E.state["bank"][bk] = t_ev
        E.state["vslot"][slot] = t_ev
    slot_p = (s * 4 + 4) % 5
    prev_vp = [None, None]
    pvm = []
    for dc in range(8):
        buf = E.wbv[dc % 2]
        w = [prev_v[dc % 2]] if dc < 2 else [prev_vp[dc % 2]]
        t_ld = b.dma("y", "dv" + str(dc % 2), buf[:, :],
                     vap_dram(E.s_wv, dc * 131072, [[1024, 128], [1, 1024]]),
                     waits=w, tag=f"vpl{dc}")
        for nn in range(2):
            bk = nn
            t_mm = b.op("t", lambda c, dc=dc, nn=nn, buf=buf, bk=bk: c.matmul(
                bank(E, bk)[0:32, :],
                E.xnr[:, dc * RING + 512: dc * RING + 544],
                buf[:, nn * 512:(nn + 1) * 512],
                start=(dc == 0), stop=(dc == 7)),
                waits=[t_ld, tok_xn,
                       E.state["bank"][bk] if dc == 0 else ("t", 0)],
                tag=f"vp{dc}")
            if dc == 7:
                pvm.append(t_mm)
        prev_vp[dc % 2] = t_mm
    for nn in range(2):
        t_ev = b.op("v", lambda c, nn=nn, slot_p=slot_p: c.tensor_copy(
            E.vsl[0:32, slot_p * 1024 + nn * 512: slot_p * 1024 + (nn + 1) * 512],
            bank(E, nn)[0:32, :]),
            waits=[pvm[nn], E.state["vslot_free"][slot_p]], tag="vpev")
        E.state["bank"][nn] = t_ev
    E.state["vslot"][slot_p] = t_ev
    E.tok_vpartial = t_ev

    if s == 0:
        prev = [E.tok_vpartial, E.tok_vpartial]
        for dc in range(8):
            buf = E.wbv[dc % 2]
            t_ld = b.dma("y", "dv" + str(dc % 2), buf[:, :],
                         vap_dram(E.s_wv, dc * 131072, [[1024, 128], [1, 1024]]),
                         waits=[prev[dc % 2]], tag=f"bvl{dc}")
            for nn in range(2):
                t_mm = b.op("t", lambda c, dc=dc, nn=nn, buf=buf: c.matmul(
                    bank(E, 2 + nn)[0:1, :], E.xn0[:, dc:dc + 1],
                    buf[:, nn * 512:(nn + 1) * 512],
                    start=(dc == 0), stop=(dc == 7)),
                    waits=[t_ld, E.tok_xn0,
                           E.state["bank"][2 + nn] if dc == 0 else ("t", 0)],
                    tag=f"bv{dc}")
            prev[dc % 2] = t_mm
        t1 = b.op("v", lambda c: c.tensor_copy(
            E.vbos[0:1, 0:512], bank(E, 2)[0:1, :]), waits=[t_mm])
        t2 = b.op("v", lambda c: c.tensor_copy(
            E.vbos[0:1, 512:1024], bank(E, 3)[0:1, :]), waits=[t_mm])
        E.state["bank"][2] = t1
        E.state["bank"][3] = t2
        E.tok_vbos = t2
        t_vd = b.dma("y", "dcn", vap_dram(E.d_vb, 0, [[1, 1024]]),
                     E.vbos[0:1, :], waits=[t1, t2], tag="vbd")
        E.tok_vbosT = b.dma("y", "dcn",
                            vap(E.vbosT, 0, [[8, 128], [1, 8]]),
                            vap_dram(E.d_vb, 0, [[1, 128], [128, 8]]),
                            waits=[t_vd], tag="vbosT", slow=True)


def emit_bos_ln0(E):
    b, ps = E.b, E.ps
    t_x0 = b.dma("y", "dcn", vap(E.x0c, 0, [[8, 128], [1, 8]]),
                 vap_dram(E.x_in, 0, [[T, 128], [128 * T, 8]]), tag="x0",
                 slow=True)
    t_r1 = b.op("v", lambda c: c.reduce_sum(E.smal[:, 6:7], E.x0c[:], AX),
                waits=[t_x0])
    t_sq = b.op("v", lambda c: c.tensor_tensor(
        E.smal[:, 8:16], E.x0c[:], E.x0c[:], ALU.mult), waits=[t_x0])
    t_r2 = b.op("v", lambda c: c.reduce_sum(E.smal[:, 7:8], E.smal[:, 8:16], AX),
                waits=[t_sq])
    t_mm = b.op("t", lambda c: c.matmul(
        bank(E, 6, 0, 2)[0:1, :], E.consts[:, 0:1], E.smal[:, 6:8],
        start=True, stop=True),
        waits=[t_r1, t_r2, E.tok_consts, E.state["bank"][6]], tag="bos_s")
    t_1 = b.op("v", lambda c: c.tensor_scalar(
        E.smal[0:1, 0:2], bank(E, 6, 0, 2)[0:1, :], 1.0 / 1024.0, None,
        ALU.mult), waits=[t_mm])
    E.state["bank"][6] = t_1
    t_2 = b.op("v", lambda c: c.tensor_tensor(
        E.smal[0:1, 2:3], E.smal[0:1, 0:1], E.smal[0:1, 0:1], ALU.mult),
        waits=[t_1])
    t_3 = b.op("v", lambda c: c.tensor_tensor(
        E.smal[0:1, 1:2], E.smal[0:1, 1:2], E.smal[0:1, 2:3], ALU.subtract),
        waits=[t_2])
    t_4 = b.op("s", lambda c: c.activation(
        E.smal[0:1, 2:3], E.smal[0:1, 1:2], ACT.Ln, bias=E.consts[0:1, 1:2],
        scale=1.0), waits=[t_3])
    t_5 = b.op("s", lambda c: c.activation(
        E.smal[0:1, 1:2], E.smal[0:1, 2:3], ACT.Exp, bias=0.0, scale=-0.5),
        waits=[t_4])
    t_bc = b.op("t", lambda c: c.matmul(
        bank(E, 6, 16, 18), E.onesf[0:1, :], E.smal[0:1, 0:2],
        start=True, stop=True), waits=[t_5, t_1], tag="bosbc")
    t_cp = b.op("v", lambda c: c.tensor_copy(
        E.smal[:, 4:6], vap(E.ps, 6 * 512 + 16, [[4096, 128], [1, 2]])),
        waits=[t_bc])
    E.state["bank"][6] = t_cp
    E.tok_xn0 = b.op("v", lambda c: c.tensor_scalar(
        E.xn0[:], E.x0c[:], E.smal[:, 4:5], E.smal[:, 5:6],
        ALU.subtract, ALU.mult), waits=[t_cp], tag="xn0")
    if "bsm" in E.taps:
        tt = b.dma("y", "dout", E.taps.pop("bsm"), E.smal[:, 0:16],
                   waits=[E.tok_xn0, t_r2, t_5], tag="tap_bsm")
        E.state["out_toks"].append(tt)


def emit_attention_qtile(E, s, jl):
    b, ps = E.b, E.ps
    st = E.state
    j = 4 * s + jl
    itl = j % 2
    slotj = j % 5
    slotj1 = (j + 1) % 5

    kst_w = [E.tok_krng, getattr(E, "tok_kst_free", None), E.tok_kbos,
             E.tok_krz if s == 0 else None]
    t_st1 = b.op("v", lambda c, jl=jl: c.tensor_copy(
        vap(E.kst, 0, [[8 * KEYS, 128], [KEYS, 8], [1, 160]]),
        vap(E.krng, jl * 128, [[8 * RING, 128], [RING, 8], [1, 160]])),
        waits=kst_w, tag="kst")
    t_st2 = b.op("v", lambda c: c.tensor_copy(
        vap(E.kst, 160, [[8 * KEYS, 128], [KEYS, 8]]),
        vap(E.kbos, 0, [[8, 128], [1, 8]])), waits=kst_w + [t_st1], tag="kstb")
    tok_kst = t_st2

    mA = E.maskA0 if j == 0 else E.maskA
    exp2_toks = []
    for r in range(2):
        mm_last = None
        for i in range(8):
            h = 8 * r + i
            c8, half = h // 2, h % 2
            base = 64 * half
            qs = c8 * ST + jl * 128

            def mmA(c, i=i, c8=c8, base=base, qs=qs):
                return c.matmul(
                    E.ps[0:128, i * 512: i * 512 + 128],
                    E.kst[base:base + 64, c8 * KEYS: c8 * KEYS + 128],
                    E.qfm[base:base + 64, qs: qs + 128],
                    start=True, stop=False)
            b.op("t", mmA, waits=[tok_kst, E.tok_qfm, st["bank"][i]],
                 tag=f"scA{h}")
        for i in range(8):
            h = 8 * r + i
            c8, half = h // 2, h % 2
            base = 64 * half
            qs = c8 * ST + jl * 128

            def mmB(c, i=i, c8=c8, base=base, qs=qs):
                return c.matmul(
                    E.ps[0:33, i * 512 + 128: i * 512 + 256],
                    E.kst[base:base + 64, c8 * KEYS + 128: (c8 + 1) * KEYS],
                    E.qfm[base:base + 64, qs: qs + 128],
                    start=False, stop=False)
            b.op("t", mmB, tag=f"scB{h}")
        for i in range(8):
            def mmMB(c, i=i):
                return c.matmul(
                    E.ps[0:33, i * 512 + 128: i * 512 + 256], E.maskB[:, :],
                    E.identb[:, :], start=False, stop=False)
            b.op("t", mmMB, waits=[E.tok_consts], tag="maskB")
        for i in range(8):
            def mmMA(c, i=i, mA=mA):
                return c.matmul(
                    E.ps[0:128, i * 512: i * 512 + 128], mA[:, :],
                    E.identb[:, :], start=False, stop=True)
            t_msk = b.op("t", mmMA, tag="maskA")
        e_w = [t_msk, getattr(E, "tok_eA_free", None),
               getattr(E, "tok_eB_free", None), E.tok_eBz]
        t_e1a = b.op("s", lambda c, r=r: c.activation(
            vap(E.eA, r * 8 * 128, [[2048, 128], [128, 8], [1, 128]]),
            vap(E.ps, 0, [[4096, 128], [512, 8], [1, 128]]),
            ACT.Exp, bias=0.0, scale=SCALE), waits=e_w, tag="exp1a")
        t_e1b = b.op("s", lambda c, r=r: c.activation(
            vap(E.eB, r * 8 * 128, [[2048, 33], [128, 8], [1, 128]]),
            vap(E.ps, 128, [[4096, 33], [512, 8], [1, 128]]),
            ACT.Exp, bias=0.0, scale=SCALE), waits=e_w, tag="exp1b")
        for i in range(8):
            h = 8 * r + i

            def mmDA(c, i=i, h=h):
                return c.matmul(
                    E.ps[0:16, i * 512 + 256: i * 512 + 384],
                    E.onesb[0:128, 0:16], E.eA[:, h * 128:(h + 1) * 128],
                    start=True, stop=False)
            b.op("t", mmDA, waits=[t_e1a, t_e1b], tag="denA")

            def mmDB(c, i=i, h=h):
                return c.matmul(
                    E.ps[0:16, i * 512 + 256: i * 512 + 384],
                    E.onesb[0:33, 0:16], E.eB[0:33, h * 128:(h + 1) * 128],
                    start=False, stop=True)
            t_den = b.op("t", mmDB, waits=[t_e1b], tag="denB")
        t_ln = b.op("s", lambda c: c.activation(
            vap(E.negL, 0, [[1024, 16], [128, 8], [1, 128]]),
            vap(E.ps, 256, [[4096, 16], [512, 8], [1, 128]]),
            ACT.Ln, bias=0.0, scale=1.0),
            waits=[t_den, getattr(E, "tok_negL_free", None)], tag="lnden")
        t_rr = b.op("s", lambda c: c.activation(
            E.negL[:, :], E.negL[:, :], ACT.Exp, bias=0.0, scale=-1.0),
            waits=[t_ln], tag="rden")
        for i in range(8):
            def mmRB(c, i=i):
                return c.matmul(
                    E.ps[0:128, i * 512: i * 512 + 128], E.onesf[0:1, 0:128],
                    E.negL[0:1, i * 128:(i + 1) * 128], start=True, stop=True)
            t_rb = b.op("t", mmRB, waits=[t_rr], tag="rbc")
        for i in range(8):
            h = 8 * r + i
            t_pa = b.op("v", lambda c, i=i, h=h: c.tensor_tensor(
                E.eA[:, h * 128:(h + 1) * 128],
                E.eA[:, h * 128:(h + 1) * 128],
                E.ps[0:128, i * 512: i * 512 + 128], ALU.mult),
                waits=[t_rb], tag=f"pa{h}")
            t_pb = b.op("v", lambda c, i=i, h=h: c.tensor_tensor(
                E.eB[0:33, h * 128:(h + 1) * 128],
                E.eB[0:33, h * 128:(h + 1) * 128],
                E.ps[0:33, i * 512: i * 512 + 128], ALU.mult),
                waits=[t_rb], tag=f"pb{h}")
        t_nl = t_pb
        E.tok_negL_free = t_rb
        exp2_toks += [t_pa, t_pb]
        for i in range(8):
            st["bank"][i] = t_pb
    E.tok_kst_free = t_nl
    if jl == NQT - 1:
        E.tok_qfm_free = t_nl
        E.tok_krng_free = tok_kst

    de_free = getattr(E, "tok_de_free", [None, None])
    t_l1a = b.dma("s", "dla" + str(itl), vap_dram(E.d_e[itl], 0, [[2048, 128], [1, 2048]]),
                  E.eA[:, :], waits=exp2_toks + [de_free[itl]], tag="l1a")
    t_l1b = b.dma("s", "dlb" + str(itl),
                  vap_dram(E.d_e[itl], 128 * 2048, [[2048, 40], [1, 2048]]),
                  E.eB[:, :], waits=exp2_toks + [de_free[itl]], tag="l1b")
    E.tok_eA_free = t_l1a
    E.tok_eB_free = t_l1b
    ph_free = getattr(E, "tok_ph_free", None)
    t_l2a = b.dma("s", "dph" + str(itl), vap(E.PH, 0, [[2688, 128], [128, 16], [1, 128]]),
                  vap_dram(E.d_e[itl], 0, [[128, 128], [16384, 16], [1, 128]]),
                  waits=[t_l1a, t_l1b, ph_free], tag="l2a")
    t_l2b = b.dma("s", "dph" + str(itl),
                  vap(E.PH, 16 * 128, [[2688, 128], [128, 5], [1, 128]]),
                  vap_dram(E.d_e[itl], 128 * 2048,
                           [[128, 128], [16384, 5], [1, 128]]),
                  waits=[t_l1b, ph_free], tag="l2b")
    tok_ph = [t_l2a, t_l2b]

    mix_tok = [None] * 6
    for grp in range(NG):
        bk = grp // 4
        col = (grp % 4) * 128

        def mmX(c, grp=grp, bk=bk, col=col):
            return c.matmul(
                E.ps[0:128, bk * 512 + col: bk * 512 + col + 128],
                E.wbig[:, :], E.PH[:, grp * 128:(grp + 1) * 128],
                start=True, stop=True)
        mix_tok[bk] = b.op("t", mmX,
                           waits=tok_ph + [E.tok_wbig, st["bank"][bk]],
                           tag=f"mix{grp}")
    E.tok_ph_free = mix_tok[5]
    mev = []
    for bk in range(6):
        ncol = 512 if bk < 5 else 128

        def ev(c, bk=bk, ncol=ncol):
            return c.tensor_copy(
                E.Msb[:, bk * 512: bk * 512 + ncol],
                E.ps[0:128, bk * 512: bk * 512 + ncol])
        t_me = b.op("v", ev, waits=[mix_tok[bk],
                                    getattr(E, "tok_msb_free", None)],
                    tag=f"mev{bk}")
        st["bank"][bk] = t_me
        mev.append(t_me)

    dm_free = getattr(E, "tok_dm_free", [None, None])
    t_m1 = b.dma("s", "dmm" + str(itl),
                 vap_dram(E.d_m[itl], 0, [[128, 128], [16384, NG], [1, 128]]),
                 vap(E.Msb, 0, [[2688, 128], [128, NG], [1, 128]]),
                 waits=mev + [dm_free[itl]], tag="m1")
    E.tok_msb_free = t_m1
    mg_free = getattr(E, "tok_mg_free", None)
    t_m2a = b.dma("s", "dmg" + str(itl), vap(E.MgA, 0, [[2048, 128], [128, 16], [1, 128]]),
                  vap_dram(E.d_m[itl], 0, [[2048, 128], [128, 16], [1, 128]]),
                  waits=[t_m1, mg_free], tag="m2a")
    t_m2b = b.dma("s", "dmg" + str(itl), vap(E.MgB, 0, [[2048, 32], [128, 16], [1, 128]]),
                  vap_dram(E.d_m[itl], 16 * 16384,
                           [[2048, 32], [128, 16], [1, 128]]),
                  waits=[t_m1, mg_free], tag="m2b")
    t_m2c = b.dma("s", "dmg" + str(itl), vap(E.Mgb, 0, [[2048, 8], [128, 16], [1, 128]]),
                  vap_dram(E.d_m[itl], 20 * 16384,
                           [[2048, 8], [128, 16], [1, 128]]),
                  waits=[t_m1, mg_free], tag="m2c")
    E.tok_de_free = de_free
    de_free[itl] = t_l2b
    dm_free[itl] = t_m2c
    E.tok_dm_free = dm_free

    av_last = None
    for ti in range(8):
        bk = 6 + (ti % 2)
        first = True
        for gg in (2 * ti, 2 * ti + 1):
            base = 64 * (gg % 2)

            def av1(c, ti=ti, gg=gg, base=base, bk=bk, first=first):
                return c.matmul(
                    E.ps[base:base + 64, bk * 512: bk * 512 + 128],
                    E.vsl[0:128, slotj * 1024 + gg * 64: slotj * 1024 + gg * 64 + 64],
                    E.MgA[:, gg * 128:(gg + 1) * 128],
                    start=True, stop=False)
            b.op("t", av1, waits=[t_m2a, t_m2b, t_m2c, st["vslot"][slotj],
                                  st["vslot"][slotj1], st["bank"][bk],
                                  (E.tok_vbos if s == 0 or True else None)],
                 tag=f"av1_{ti}_{gg}")
            first = False

            def av2(c, gg=gg, base=base, bk=bk):
                return c.matmul(
                    E.ps[base:base + 64, bk * 512: bk * 512 + 128],
                    E.vsl[0:32, slotj1 * 1024 + gg * 64: slotj1 * 1024 + gg * 64 + 64],
                    E.MgB[0:32, gg * 128:(gg + 1) * 128],
                    start=False, stop=False)
            b.op("t", av2, tag=f"av2_{gg}")

            def av3(c, gg=gg, base=base, bk=bk):
                return c.matmul(
                    E.ps[base:base + 64, bk * 512: bk * 512 + 128],
                    E.vbos[0:1, gg * 64: gg * 64 + 64],
                    E.Mgb[0:1, gg * 128:(gg + 1) * 128],
                    start=False, stop=True)
            t_av = b.op("t", av3, tag=f"av3_{gg}")
        t_ae = b.op("v", lambda c, ti=ti, bk=bk, jl=jl: c.tensor_copy(
            E.atto[:, 4096 + ti * 512 + jl * 128: 4096 + ti * 512 + jl * 128 + 128],
            E.ps[0:128, bk * 512: bk * 512 + 128]),
            waits=[t_av, getattr(E, "tok_atto_free", None)], tag=f"atev{ti}")
        st["bank"][bk] = t_ae
        av_last = t_ae
    st["vslot_free"][slotj] = t_av
    st["vslot_free"][slotj1] = t_av
    E.tok_mg_free = t_av
    if jl == 0:
        E.tok_atto = []
    E.tok_atto.append(av_last)


def emit_bos_kv(E):
    pass


def emit_outproj(E, s):
    b, ps = E.b, E.ps
    st = E.state
    x = E.xb[s % 2]
    prev = [None, None]
    x2toks = []
    for dc in range(8):
        buf = E.wbo[dc % 2]
        w = [prev[dc % 2]] if prev[dc % 2] else []
        t_ld = b.dma("y", "dwo" + str(dc % 2), buf[:, :],
                     vap_dram(E.s_wo, dc * 8 * 16384, [[1024, 128], [1, 1024]]),
                     waits=w, tag=f"wol{dc}")
        bk = dc % 2
        for ic in range(8):
            def mmO(c, dc=dc, ic=ic, buf=buf, bk=bk):
                return c.matmul(
                    bank(E, bk), buf[:, ic * 128:(ic + 1) * 128],
                    E.atto[:, 4096 + ic * ST: 4096 + (ic + 1) * ST],
                    start=(ic == 0), stop=(ic == 7))
            t_mm = b.op("t", mmO,
                        waits=[t_ld] + E.tok_atto +
                        ([st["bank"][bk]] if ic == 0 else []), tag=f"op{dc}")
        if s == 0:
            for ic in range(8):
                def mmD0(c, dc=dc, ic=ic, buf=buf):
                    return c.matmul(
                        E.ps[0:128, 4 * 512 + dc: 4 * 512 + dc + 1],
                        buf[:, ic * 128:(ic + 1) * 128],
                        E.vbosT[:, ic:ic + 1],
                        start=(dc == 0 and ic == 0),
                        stop=(dc == 7 and ic == 7))
                t_d0 = b.op("t", mmD0,
                            waits=[E.tok_vbosT] +
                            ([st["bank"][4]] if dc == 0 and ic == 0 else []),
                            tag="d0")
            if dc == 7:
                E.tok_d0 = t_d0
            prev[dc % 2] = t_d0
        else:
            prev[dc % 2] = t_mm
        t_ev = b.op("v", lambda c, dc=dc, bk=bk: c.scalar_tensor_tensor(
            E.x2r[:, dc * R2 + 2: dc * R2 + 2 + ST], bank(E, bk),
            E.bout[:, dc:dc + 1], x[:, dc * ST:(dc + 1) * ST],
            ALU.add, ALU.add),
            waits=[t_mm, st["xtok"][s % 2],
                   getattr(E, "tok_x2_free", None)], tag=f"opev{dc}")
        st["bank"][bk] = t_ev
        x2toks.append(t_ev)
    E.tok_x2 = x2toks
    E.tok_atto_free = x2toks[-1]


def emit_bos_x2(E):
    b = E.b
    st = E.state
    toks = []
    for dc in range(8):
        t = b.op("v", lambda c, dc=dc: c.scalar_tensor_tensor(
            E.x2r[:, dc * R2 + 1: dc * R2 + 2],
            E.ps[0:128, 4 * 512 + dc: 4 * 512 + dc + 1],
            E.bout[:, dc:dc + 1], E.x0c[:, dc:dc + 1], ALU.add, ALU.add),
            waits=[E.tok_d0, E.tok_x2z], tag="bosx2")
        toks.append(t)
    st["bank"][4] = toks[-1]
    E.tok_x2bos = toks


def emit_ln2_quant1(E, s):
    b, ps = E.b, E.ps
    st = E.state
    u = E.xb[s % 2]
    x2v = vap(E.x2r, 2, [[8 * R2, 128], [R2, 8], [1, ST]])

    t_cast = b.op("v", lambda c: c.tensor_copy(
        vap(E.stat, 0, [[2 * 8 * ST, 128], [ST, 8], [1, ST]]), x2v),
        waits=E.tok_x2 + [E.tok_stat_free,
                          getattr(E, "tok_qfm_free", None)] +
        (E.tok_x2bos if s == 0 else []), tag="ln2cast")
    for c8 in range(8):
        t_s1 = b.op("t", lambda c, c8=c8: c.matmul(
            bank(E, 2), E.onesb[:, :], E.stat[:, c8 * ST:(c8 + 1) * ST],
            start=(c8 == 0), stop=(c8 == 7)),
            waits=[t_cast, st["bank"][2]], tag="ln2s1")
    t_m = b.op("v", lambda c: c.tensor_scalar(
        E.m_bc[:], bank(E, 2), 1.0 / 1024.0, None, ALU.mult),
        waits=[t_s1, getattr(E, "tok_bc_free", None)], tag="ln2m")
    st["bank"][2] = t_m
    t_sq = b.op("v", lambda c: c.tensor_tensor(
        vap(E.stat, 0, [[2 * 8 * ST, 128], [ST, 8], [1, ST]]), x2v, x2v, ALU.mult),
        waits=[t_s1], tag="ln2sq")
    for c8 in range(8):
        t_s2 = b.op("t", lambda c, c8=c8: c.matmul(
            bank(E, 3), E.onesb[:, :], E.stat[:, c8 * ST:(c8 + 1) * ST],
            start=(c8 == 0), stop=(c8 == 7)),
            waits=[t_sq, st["bank"][3]], tag="ln2s2")
    E.tok_stat_free = t_s2
    t_v0 = b.op("v", lambda c: c.tensor_scalar(
        E.s_bc[:], bank(E, 3), 1.0 / 1024.0, None, ALU.mult), waits=[t_s2])
    st["bank"][3] = t_v0
    t_mm2 = b.op("v", lambda c: c.tensor_tensor(
        E.a_bc[:], E.m_bc[:], E.m_bc[:], ALU.mult), waits=[t_m])
    t_var = b.op("v", lambda c: c.tensor_tensor(
        E.s_bc[:], E.s_bc[:], E.a_bc[:], ALU.subtract), waits=[t_v0, t_mm2])
    t_lv = b.op("s", lambda c: c.activation(
        E.a_bc[:], E.s_bc[:], ACT.Ln, bias=E.consts[:, 1:2], scale=1.0),
        waits=[t_var], tag="ln2lv")

    utoks = []
    for c8 in range(8):
        t_u = b.op("v", lambda c, c8=c8: c.tensor_tensor(
            u[:, c8 * ST:(c8 + 1) * ST],
            E.x2r[:, c8 * R2 + 2: c8 * R2 + 2 + ST], E.m_bc[:], ALU.subtract),
            waits=[t_m] + E.tok_x2, tag=f"u{c8}")
        utoks.append(t_u)
    t_mr = b.op("v", lambda c: c.tensor_reduce(
        E.m_bc[:], vap(E.xb[s % 2], 0, [[4096, 128], [1, ST], [ST, 8]]),
        AX, ALU.max, apply_absolute_value=True), waits=utoks, tag="q1max")
    for hi in range(4):
        tq1 = b.op("t", lambda c, hi=hi: c.transpose(
            bank(E, 2, hi * 128, hi * 128 + 128),
            E.m_bc[:, hi * 128:(hi + 1) * 128], E.identf[:]),
            waits=[t_mr, E.tok_consts] +
            ([E.state["bank"][2]] if hi == 0 else []), tag=f"q1t{hi}")
    tq3 = b.op("v", lambda c: c.tensor_reduce(
        E.smal[:, 8:12], vap(E.ps, 2 * 512, [[4096, 128], [128, 4], [1, 128]]),
        AX, ALU.max), waits=[tq1], tag="q1red")
    E.state["bank"][2] = tq3
    for hi in range(4):
        tq4 = b.op("t", lambda c, hi=hi: c.transpose(
            bank(E, 3, hi * 128, hi * 128 + 128)[0:1, :],
            E.smal[:, 8 + hi:9 + hi], E.identf[:]),
            waits=[tq3] + ([E.state["bank"][3]] if hi == 0 else []),
            tag=f"q1u{hi}")
    tq5 = b.op("v", lambda c: c.tensor_copy(
        E.negL[0:1, 0:512], bank(E, 3)[0:1, :]),
        waits=[tq4, getattr(E, "tok_negL_free", None)], tag="q1rw")
    E.state["bank"][3] = tq5
    t_bcm = b.op("t", lambda c: c.matmul(
        bank(E, 2), E.onesf[0:1, :], E.negL[0:1, 0:512],
        start=True, stop=True), waits=[tq5, E.state["bank"][2]], tag="q1bc")
    E.state["bank"][2] = t_bcm
    E.tok_negL_free = t_bcm
    t_lm = b.op("s", lambda c: c.activation(
        E.m_bc[:], bank(E, 2), ACT.Ln, bias=0.0, scale=1.0), waits=[t_bcm])
    E.state["bank"][2] = t_lm
    t_s1z = b.op("s", lambda c: c.activation(
        E.s_bc[:], E.m_bc[:], ACT.Exp, bias=E.consts[:, 2:3], scale=-1.0),
        waits=[t_lm], tag="s1z")
    t_e3 = b.op("v", lambda c: c.scalar_tensor_tensor(
        E.a_bc[:], E.a_bc[:], -0.5, E.m_bc[:], ALU.mult, ALU.add),
        waits=[t_lv, t_lm], tag="q1e3")
    t_al = b.op("s", lambda c: c.activation(
        E.alr[:, 2:2 + ST], E.a_bc[:], ACT.Exp, bias=E.lnc1[:], scale=1.0),
        waits=[t_e3, E.tok_lnc[0], getattr(E, "tok_alr_free", None)],
        tag="alpha")
    E.tok_alr = t_al
    xq1toks = []
    for c8 in range(8):
        t_z = b.op("v", lambda c, c8=c8: c.tensor_tensor(
            u[:, c8 * ST:(c8 + 1) * ST], u[:, c8 * ST:(c8 + 1) * ST],
            E.s_bc[:], ALU.mult), waits=[t_s1z], tag=f"z1{c8}")
        t_q = b.op("v", lambda c, c8=c8: c.tensor_scalar(
            E.xq1r[:, c8 * R2 + 2: c8 * R2 + 2 + ST],
            u[:, c8 * ST:(c8 + 1) * ST], E.cdom[:], E.cdom[:],
            ALU.add, ALU.subtract),
            waits=[t_z, E.tok_cdom, getattr(E, "tok_xq1_free", None)],
            tag=f"q1r{c8}")
        xq1toks.append(t_q)
    E.tok_xq1 = xq1toks
    E.tok_bc_free = t_al
    E.tok_ufree = xq1toks


def emit_bos_quant1(E):
    b, ps = E.b, E.ps
    st = E.state
    xb8 = AP(E.x2r[:].tensor, 1, [[8 * R2, 128], [R2, 8]])
    t_r1 = b.op("v", lambda c: c.reduce_sum(E.smal[:, 6:7], xb8, AX),
                waits=E.tok_x2bos)
    t_sq = b.op("v", lambda c: c.tensor_tensor(
        E.smal[:, 8:16], xb8, xb8, ALU.mult), waits=E.tok_x2bos)
    t_r2 = b.op("v", lambda c: c.reduce_sum(E.smal[:, 7:8], E.smal[:, 8:16],
                                            AX), waits=[t_sq])
    t_mm = b.op("t", lambda c: c.matmul(
        bank(E, 5, 0, 2)[0:1, :], E.consts[:, 0:1], E.smal[:, 6:8],
        start=True, stop=True), waits=[t_r1, t_r2, st["bank"][5]], tag="bq_s")
    t_1 = b.op("v", lambda c: c.tensor_scalar(
        E.smal[0:1, 0:2], bank(E, 5, 0, 2)[0:1, :], 1.0 / 1024.0, None,
        ALU.mult), waits=[t_mm])
    st["bank"][5] = t_1
    t_2 = b.op("v", lambda c: c.tensor_tensor(
        E.smal[0:1, 2:3], E.smal[0:1, 0:1], E.smal[0:1, 0:1], ALU.mult),
        waits=[t_1])
    t_3 = b.op("v", lambda c: c.tensor_tensor(
        E.smal[0:1, 1:2], E.smal[0:1, 1:2], E.smal[0:1, 2:3], ALU.subtract),
        waits=[t_2])
    t_4 = b.op("s", lambda c: c.activation(
        E.smal[0:1, 2:3], E.smal[0:1, 1:2], ACT.Ln, bias=E.consts[0:1, 1:2],
        scale=1.0), waits=[t_3])
    t_5 = b.op("s", lambda c: c.activation(
        E.smal[0:1, 1:2], E.smal[0:1, 2:3], ACT.Exp, bias=0.0, scale=-0.5),
        waits=[t_4])
    t_bc = b.op("t", lambda c: c.matmul(
        bank(E, 5, 16, 18), E.onesf[0:1, :], E.smal[0:1, 0:2],
        start=True, stop=True), waits=[t_5, t_1], tag="bqbc")
    t_cp = b.op("v", lambda c: c.tensor_copy(
        E.smal[:, 4:6], vap(E.ps, 5 * 512 + 16, [[4096, 128], [1, 2]])),
        waits=[t_bc])
    st["bank"][5] = t_cp
    t_u = b.op("v", lambda c: c.tensor_scalar(
        E.smal[:, 8:16], xb8, E.smal[:, 4:5], None, ALU.subtract),
        waits=[t_cp], tag="bqu")
    t_rm = b.op("v", lambda c: c.reduce_max(
        E.smal[:, 6:7], E.smal[:, 8:16], AX, apply_absolute_value=True),
        waits=[t_u])
    tb1 = b.op("t", lambda c: c.transpose(
        bank(E, 5, 128, 256)[0:1, :], E.smal[:, 6:7], E.identf[:]),
        waits=[t_rm, E.tok_consts, st["bank"][5]], tag="bqt")
    tb2 = b.op("v", lambda c: c.tensor_copy(
        E.negL[0:1, 0:128], bank(E, 5, 128, 256)[0:1, :]),
        waits=[tb1, getattr(E, "tok_negL_free", None)], tag="bqrw")
    st["bank"][5] = tb2
    E.tok_negL_free = tb2
    t_gm = b.op("v", lambda c: c.reduce_max(
        E.smal[0:1, 7:8], E.negL[0:1, 0:128], AX), waits=[tb2], tag="bqmax")
    t_rc = b.op("v", lambda c: c.reciprocal(E.smal[0:1, 6:7],
                                            E.smal[0:1, 7:8]), waits=[t_gm])
    t_s1 = b.op("v", lambda c: c.tensor_scalar(
        E.smal[0:1, 6:7], E.smal[0:1, 6:7], 127.0, None, ALU.mult),
        waits=[t_rc])
    t_ab1 = b.op("v", lambda c: c.tensor_tensor(
        E.smal[0:1, 7:8], E.smal[0:1, 7:8], E.smal[0:1, 1:2], ALU.mult),
        waits=[t_gm, t_5], tag="bqab1")
    t_ab2 = b.op("v", lambda c: c.tensor_scalar(
        E.smal[0:1, 7:8], E.smal[0:1, 7:8], 1.0 / 127.0, None, ALU.mult),
        waits=[t_ab1])
    t_ab3 = b.op("v", lambda c: c.tensor_tensor(
        E.smal[0:1, 7:8], E.smal[0:1, 7:8], E.swsc[0:1, 0:1], ALU.mult),
        waits=[t_ab2], tag="bqab3")
    t_bq = b.op("t", lambda c: c.matmul(
        bank(E, 5, 32, 34), E.onesf[0:1, :], E.smal[0:1, 6:8],
        start=True, stop=True), waits=[t_s1, t_ab3, st["bank"][5]],
        tag="bqbc2")
    t_cp2 = b.op("v", lambda c: c.tensor_copy(
        E.smal[:, 4:6], vap(E.ps, 5 * 512 + 32, [[4096, 128], [1, 2]])),
        waits=[t_bq])
    st["bank"][5] = t_cp2
    t_z = b.op("v", lambda c: c.tensor_scalar(
        E.smal[:, 8:16], E.smal[:, 8:16], E.smal[:, 4:5], E.cdom[:],
        ALU.mult, ALU.add), waits=[t_cp2, E.tok_cdom], tag="bqz")
    t_q = b.op("v", lambda c: c.tensor_scalar(
        AP(E.xq1r[:].tensor, 1, [[8 * R2, 128], [R2, 8]]),
        E.smal[:, 8:16], E.cdom[:], None, ALU.subtract),
        waits=[t_z, E.tok_xq1z], tag="bqq")
    E.tok_xq1bos = t_q
    t_a3 = b.op("v", lambda c: c.tensor_copy(
        E.alr[:, 1:2], E.smal[:, 5:6]), waits=[t_cp2, E.tok_alz], tag="bqa3")
    E.tok_albos = t_a3


def emit_ffn_subtile(E, s, u):
    b, ps = E.b, E.ps
    st = E.state
    base = 256 * u

    w_xq1 = E.tok_xq1 + [E.tok_xq1bos if s == 0 else None,
                         getattr(E, "tok_xq1tail", None)]
    w_alr = [E.tok_alr, E.tok_albos if s == 0 else None,
             getattr(E, "tok_altail", None)]

    prev_w = [None, None]
    acc_free = [None, None, None]
    silu_toks = []
    hrow_toks = []
    for hc in range(32):
        buf = E.wb1[hc % 2]
        w = [prev_w[hc % 2]] if prev_w[hc % 2] else ([t for t in E.tok_prep]
                                                     if s == 0 and u == 0 and hc < 2 else [])
        t_ld = b.dma("y", "dw1" + str(hc % 2), buf[:, :],
                     vap_dram(E.s_w1, hc * 8 * 16384, [[1024, 128], [1, 1024]]),
                     waits=w, tag=f"w1l{hc}")
        bk = hc % 4
        for dc in range(8):
            def mmH(c, hc=hc, dc=dc, buf=buf, bk=bk, base=base):
                return c.matmul(
                    E.ps[0:128, bk * 512: bk * 512 + 258],
                    buf[:, dc * 128:(dc + 1) * 128],
                    E.xq1r[:, dc * R2 + base: dc * R2 + base + 258],
                    start=(dc == 0), stop=(dc == 7))
            t_mm = b.op("t", mmH,
                        waits=[t_ld] + w_xq1 +
                        ([st["bank"][bk]] if dc == 0 else []), tag=f"h{hc}")
        prev_w[hc % 2] = t_mm
        hrow = E.hbuf[:, hc * (SUB + 2): hc * (SUB + 2) + 258]
        t_ev = b.op("v", lambda c, hrow=hrow, bk=bk, base=base: c.tensor_tensor(
            hrow, bank(E, bk, 0, 258), E.alr[:, base: base + 258], ALU.mult),
            waits=[t_mm] + w_alr + [getattr(E, "tok_hrow_free", None)],
            tag=f"hev{hc}")
        st["bank"][bk] = t_ev
        hrow_toks.append(t_ev)
        t_c1 = b.op("v", lambda c, hrow=hrow, hc=hc: c.tensor_scalar(
            E.accr[:, :], hrow[:, 0:256], E.convw[:, hc:hc + 1],
            E.convb[:, hc:hc + 1], ALU.mult, ALU.add),
            waits=[t_ev, E.tok_consts,
                   (silu_toks[-1] if silu_toks else None)], tag=f"cv1_{hc}")
        t_c2 = b.op("v", lambda c, hrow=hrow, hc=hc: c.scalar_tensor_tensor(
            E.accr[:, :], hrow[:, 1:257], E.convw[:, 32 + hc:33 + hc],
            E.accr[:, :], ALU.mult, ALU.add), waits=[t_c1], tag=f"cv2_{hc}")
        t_c3 = b.op("v", lambda c, hrow=hrow, hc=hc: c.scalar_tensor_tensor(
            E.accr[:, :], hrow[:, 2:258], E.convw[:, 64 + hc:65 + hc],
            E.accr[:, :], ALU.mult, ALU.add), waits=[t_c2], tag=f"cv3_{hc}")
        t_si = b.op("s", lambda c, hrow=hrow: c.activation(
            hrow[:, 1:257], E.accr[:, :], ACT.Silu, bias=0.0, scale=1.0),
            waits=[t_c3], tag=f"silu{hc}")
        silu_toks.append(t_si)

    sv = vap(E.hbuf, 1, [[32 * (SUB + 2), 128], [1, SUB], [SUB + 2, 32]])
    t_mr = b.op("v", lambda c: c.tensor_reduce(
        E.s2bc[:], sv, AX, ALU.max, apply_absolute_value=True),
        waits=silu_toks, tag="q2max")
    for hi in range(2):
        tq1 = b.op("t", lambda c, hi=hi: c.transpose(
            bank(E, 2, hi * 128, hi * 128 + 128),
            E.s2bc[:, hi * 128:(hi + 1) * 128], E.identf[:]),
            waits=[t_mr, E.tok_consts] +
            ([st["bank"][2]] if hi == 0 else []), tag=f"q2t{hi}")
    tq3 = b.op("v", lambda c: c.tensor_reduce(
        E.smal[:, 8:10], vap(E.ps, 2 * 512, [[4096, 128], [128, 2], [1, 128]]),
        AX, ALU.max), waits=[tq1], tag="q2red")
    st["bank"][2] = tq3
    for hi in range(2):
        tq4 = b.op("t", lambda c, hi=hi: c.transpose(
            bank(E, 3, hi * 128, hi * 128 + 128)[0:1, :],
            E.smal[:, 8 + hi:9 + hi], E.identf[:]),
            waits=[tq3] + ([st["bank"][3]] if hi == 0 else []),
            tag=f"q2u{hi}")
    tq5 = b.op("v", lambda c: c.tensor_copy(
        E.negL[0:1, 0:256], bank(E, 3, 0, 256)[0:1, :]),
        waits=[tq4, getattr(E, "tok_negL_free", None)], tag="q2rw")
    st["bank"][3] = tq5
    t_bcm = b.op("t", lambda c: c.matmul(
        bank(E, 1, 0, 256), E.onesf[0:1, :], E.negL[0:1, 0:256],
        start=True, stop=True), waits=[tq5, st["bank"][1]], tag="q2bc")
    E.tok_negL_free = t_bcm
    st["bank"][1] = t_bcm
    t_lm = b.op("s", lambda c: c.activation(
        E.s2bc[:], bank(E, 1, 0, 256), ACT.Ln, bias=0.0, scale=1.0),
        waits=[t_bcm])
    st["bank"][1] = t_lm
    t_s2 = b.op("s", lambda c: c.activation(
        E.beta[:], E.s2bc[:], ACT.Exp, bias=E.lnc2[:], scale=1.0),
        waits=[t_lm, E.tok_lnc[1]], tag="beta")
    t_s2z = b.op("s", lambda c: c.activation(
        E.s2bc[:], E.s2bc[:], ACT.Exp, bias=E.consts[:, 2:3], scale=-1.0),
        waits=[t_s2], tag="s2z")
    s2v = vap(E.s2bc, 0, [[SUB, 128], [1, SUB], [0, 32]])
    t_z2 = b.op("v", lambda c: c.tensor_tensor(sv, sv, s2v, ALU.mult),
                waits=[t_s2z], tag="z2")
    t_q2 = b.op("v", lambda c: c.tensor_scalar(
        vap(E.xq2, 0, [[32 * SUB, 128], [1, SUB], [SUB, 32]]), sv,
        E.cdom[:], E.cdom[:], ALU.add, ALU.subtract),
        waits=[t_z2, getattr(E, "tok_xq2_free", None),
               getattr(E, "tok_atto_free", None), E.tok_stat_free],
        tag="q2r")
    E.tok_hrow_free = t_q2

    prev_w2 = [None, None]
    for hc in range(32):
        buf = E.wb2[hc % 2]
        w = [prev_w2[hc % 2]] if prev_w2[hc % 2] else []
        t_ld = b.dma("y", "dw2" + str(hc % 2), buf[:, :],
                     vap_dram(E.s_w2, hc * 8 * 16384, [[1024, 128], [1, 1024]]),
                     waits=w, tag=f"w2l{hc}")
        for dc in range(8):
            def mm2(c, hc=hc, dc=dc, buf=buf):
                return c.matmul(
                    E.ps[0:128, dc * 512: dc * 512 + 256],
                    buf[:, dc * 128:(dc + 1) * 128],
                    E.xq2[:, hc * SUB:(hc + 1) * SUB],
                    start=(hc == 0), stop=(hc == 31))
            t_mm = b.op("t", mm2,
                        waits=[t_ld, t_q2] +
                        ([st["bank"][dc]] if hc == 0 else []), tag=f"o2_{dc}")
        prev_w2[hc % 2] = t_mm
    tok_ffn2 = t_mm
    x3 = E.xb[s % 2]
    ev_toks = []
    for dc in range(8):
        t_e1 = b.op("v", lambda c, dc=dc: c.tensor_tensor(
            x3[:, dc * SUB:(dc + 1) * SUB], bank(E, dc, 0, 256), E.beta[:],
            ALU.mult), waits=[tok_ffn2, getattr(E, "tok_out_prev", None)] +
            E.tok_ufree, tag=f"f2e{dc}")
        t_e2 = b.op("v", lambda c, dc=dc: c.tensor_tensor(
            x3[:, dc * SUB:(dc + 1) * SUB], x3[:, dc * SUB:(dc + 1) * SUB],
            E.x2r[:, dc * R2 + 1 + base: dc * R2 + 1 + base + SUB], ALU.add),
            waits=[t_e1], tag=f"f2r{dc}")
        st["bank"][dc] = t_e2
        ev_toks.append(t_e2)
    E.tok_xq2_free = tok_ffn2
    t0 = 512 * s + 256 * u
    t_out = b.dma("y", "dout",
                  vap_dram(E.out, t0, [[T, 128], [128 * T, 8], [1, SUB]]),
                  vap(E.xb[s % 2], 0, [[8 * ST, 128], [SUB, 8], [1, SUB]]),
                  waits=ev_toks, tag=f"out{s}_{u}")
    st["out_toks"].append(t_out)
    E.tok_out_prev = t_out
    if u == NSUB - 1:
        st["xfree"][s % 2] = t_out
        E.tok_ffn_done = [t_out, t_q2, tok_ffn2]


def emit_tails(E, s):
    if s == NST - 1:
        return
    b = E.b
    ffn_done = E.tok_ffn_done

    def cp(dst_t, src_off, dst_off, n, F, nch, waits, tag):
        return b.op("v", lambda c: c.tensor_copy(
            vap(dst_t, dst_off, [[8 * F, 128], [F, nch], [1, n]]),
            vap(dst_t, src_off, [[8 * F, 128], [F, nch], [1, n]])),
            waits=waits, tag=tag)

    E.tok_x2tail = cp(E.x2r, 512, 0, 2, R2, 8, ffn_done + E.tok_x2, "tlx2")
    E.tok_xq1tail = cp(E.xq1r, 512, 0, 2, R2, 8, ffn_done + E.tok_xq1, "tlq1")
    t_al = b.op("v", lambda c: c.tensor_copy(
        E.alr[:, 0:2], E.alr[:, 512:514]), waits=ffn_done + [E.tok_alr],
        tag="tlal")
    E.tok_altail = t_al
    E.tok_x2_free = E.tok_x2tail
    E.tok_ktail = cp(E.krng, 512, 0, 32, RING, 8,
                     [E.tok_krng, getattr(E, "tok_krng_free", None)], "tlk")
    E.tok_xntail = cp(E.xnr, 512, 0, 32, RING, 8,
                      [E.tok_vpartial], "tlxn")
    E.tok_xq1_free = E.tok_xq1tail
    E.tok_alr_free = t_al


def _host_consts():
    maskA = np.zeros((128, 128), np.float32)
    maskA0 = np.zeros((128, 128), np.float32)
    maskB = np.zeros((128, 33), np.float32)
    for t in range(128):
        for jw in range(128):
            dead = t < jw - 32 or jw < 16 * (t // 16)
            if dead:
                maskA[t, jw] = NEG
            maskA0[t, jw] = NEG if (dead or jw < 32) else 0.0
        for jj in range(32):
            if t < 96 + jj:
                maskB[t, jj] = NEG
    bf = ml_dtypes.bfloat16
    consts = np.zeros((128, 8), np.float32)
    consts[:, 0] = 1.0
    consts[:, 1] = 1e-5
    consts[:, 2] = LN127
    return {
        "maskA": maskA.astype(bf), "maskB": maskB.astype(bf),
        "maskA0": maskA0.astype(bf),
        "onesb": np.ones((128, 128), bf),
        "onesf": np.ones((1, 128), np.float32),
        "identb": np.eye(128, dtype=np.float32).astype(bf),
        "identf": np.eye(128, dtype=np.float32),
        "consts": consts,
    }


def kernel(L0_attn_x, L0_attn_w_qkv, L0_attn_w_out, L0_attn_b_out, L0_attn_w_th,
           L0_ffn_w1, L0_ffn_conv_w, L0_ffn_conv_b, L0_ffn_w2):
    x = np.ascontiguousarray(np.asarray(L0_attn_x, dtype=np.float32))
    wqkv = np.ascontiguousarray(np.asarray(L0_attn_w_qkv, dtype=np.float32))
    wout = np.ascontiguousarray(np.asarray(L0_attn_w_out, dtype=np.float32))
    bout = np.ascontiguousarray(
        np.asarray(L0_attn_b_out, np.float32).reshape(8, 128).T)
    wtht = np.ascontiguousarray(np.asarray(L0_attn_w_th, np.float32).T)
    w1t = np.ascontiguousarray(np.asarray(L0_ffn_w1, np.float32).T)
    w2t = np.ascontiguousarray(np.asarray(L0_ffn_w2, np.float32).T)
    cw = np.asarray(L0_ffn_conv_w, np.float32)[:, 0, :]
    convw = np.ascontiguousarray(
        cw.reshape(32, 128, 3).transpose(1, 0, 2).reshape(128, 96, order="F")
        if False else
        np.concatenate([cw[:, k].reshape(32, 128).T for k in range(3)], axis=1))
    convb = np.ascontiguousarray(
        np.asarray(L0_ffn_conv_b, np.float32).reshape(32, 128).T)
    cst = _host_consts()

    nc = build_nc(tuple(DEBUG))
    in_maps = []
    for bidx in range(8):
        m = {"x": np.ascontiguousarray(x[bidx].T),
             "wqkv": wqkv, "wout": wout, "bout": bout, "wtht": wtht,
             "w1t": w1t, "w2t": w2t, "convw": convw, "convb": convb}
        m.update(cst)
        in_maps.append(m)
    res = run_bass_kernel_spmd(nc, in_maps, core_ids=list(range(8)))
    out = np.stack([np.asarray(r["o"]).T for r in res.results], axis=0)
    if DEBUG:
        kernel.tapout = [{k: np.asarray(r["tap_" + k]) for k in DEBUG}
                         for r in res.results]
    return out
